# version 01
# baseline (speedup 1.0000x reference)
import contextlib
import math

import numpy as np
import ml_dtypes

import concourse.bass as bass
import concourse.mybir as mybir
from concourse.bass_utils import run_bass_kernel_spmd

F32 = mybir.dt.float32
BF16 = mybir.dt.bfloat16
I32 = mybir.dt.int32
AF = mybir.ActivationFunctionType
ALU = mybir.AluOpType

NC = 8
T = 8192
D = 2048
NTOK = T // NC
CH = 512
NW = NTOK // CH
NCH = T // CH
DCN = D // 128
HID0 = 5632
HC0 = HID0 // 128
HID1 = 8192
HC1 = HID1 // 128
E50 = float(np.exp(np.float64(-50.0)))
BF = ml_dtypes.bfloat16
SQ128 = float(np.sqrt(np.float64(128.0)))


def own_chunks(c):
    return [c, NCH - 1 - c]


def gblock_of_chunk(j):
    return 2 * j if j < NC else 2 * (NCH - 1 - j) + 1


class Sem:
    def __init__(self, es, nc, name):
        self.h = es.enter_context(nc.semaphore(name))
        self.n = 0

    def inc(self, amt=1):
        self.n += amt
        return self.n


class _VecSync:
    _OPS = frozenset((
        "tensor_tensor", "tensor_scalar", "tensor_copy", "memset",
        "reciprocal", "scalar_tensor_tensor", "tensor_add", "tensor_sub",
        "tensor_mul", "tensor_max", "tensor_reduce", "reduce_max",
        "reduce_sum", "tensor_scalar_add", "tensor_scalar_mul", "select",
        "tensor_single_scalar", "memzero", "tensor_relu",
    ))

    def __init__(self, k, v):
        self._k = k
        self._v = v

    def __getattr__(self, name):
        f = getattr(self._v, name)
        if name not in _VecSync._OPS:
            return f

        def wrapped(*a, **kw):
            s = self._k.sem("dve")
            if s.n:
                self._v.wait_ge(s.h, s.n)
            inst = f(*a, **kw)
            inst.then_inc(s.h, 1)
            s.inc(1)
            return inst
        return wrapped


class K:
    def __init__(self, name):
        self.nc = bass.Bass("TRN2", target_bir_lowering=False)
        self.es = contextlib.ExitStack()
        self.name = name
        self._sems = {}

    def sem(self, name) -> Sem:
        if name not in self._sems:
            self._sems[name] = Sem(self.es, self.nc, name)
        return self._sems[name]

    def sem_unique(self, prefix) -> Sem:
        if not hasattr(self, "_useq"):
            self._useq = 0
        self._useq += 1
        return self.sem(f"{prefix}_{self._useq}")

    def track_store(self, sem):
        if not hasattr(self, "store_sems"):
            self.store_sems = []
        if sem not in self.store_sems:
            self.store_sems.append(sem)

    def sb(self, name, shape, dtype):
        return self.es.enter_context(self.nc.sbuf_tensor(name, shape, dtype))

    def ps(self, name, shape, dtype):
        return self.es.enter_context(self.nc.psum_tensor(name, shape, dtype))

    def din(self, name, shape, dtype):
        return self.nc.dram_tensor(name, shape, dtype, kind="ExternalInput").ap()

    def dout(self, name, shape, dtype):
        return self.nc.dram_tensor(name, shape, dtype, kind="ExternalOutput").ap()

    def setup_common(self):
        self.block = self.es.enter_context(self.nc.Block())
        self.ps_f = [self.ps(f"psf{i}", [128, 512], F32) for i in range(6)]
        self.ident_f = self.sb("ident_f", [128, 128], F32)
        self.ident_b = self.sb("ident_b", [128, 128], BF16)
        self.ones_col_bf = self.sb("ones_col_bf", [128, 1], BF16)
        self.ones_row_f = self.sb("ones_row_f", [1, 128], F32)
        self.neg128_row = self.sb("neg128_row", [1, 128], F32)
        self.scr512 = self.sb("scr512", [128, 512], F32)
        blk = self.block
        sv = self.sem("dve")
        idf = self.din("ident_f_in", [128, 128], F32)
        idb = self.din("ident_b_in", [128, 128], BF16)
        sl = self.sem_unique("ldid")

        @blk.sync
        def _(sy):
            sy.dma_start(out=self.ident_f[:], in_=idf).then_inc(sl.h, 16)
            sy.dma_start(out=self.ident_b[:], in_=idb).then_inc(sl.h, 16)
        sl.inc(32)
        ld_done = sl.n

        @blk.vector
        def _(v):
            v = _VecSync(self, v)
            v.wait_ge(sl.h, ld_done)
            v.memset(self.ones_col_bf[:], 1.0)
            v.memset(self.ones_row_f[:], 1.0)
            v.memset(self.neg128_row[:], -128.0)
        self._ccols = {}
        self._ccol_tile = self.sb("ccols", [128, 8], F32)
        self._ccol_used = 0

    def guard_counts(self):
        return [(s, s.n) for s in self._sems.values() if s.n > 0]

    def guard(self, eng, gd=None):
        assert gd is not None, "pass guard_counts() captured at phase entry"
        for s, n in gd:
            if n:
                eng.wait_ge(s.h, n)

    def constcol(self, value):
        if value in self._ccols:
            return self._ccols[value]
        i = self._ccol_used
        assert i < 8
        self._ccol_used += 1
        ap = self._ccol_tile[:, i:i + 1]
        blk = self.block
        sv = self.sem("dve")

        @blk.vector
        def _(v):
            v = _VecSync(self, v)
            v.memset(ap, float(value))
        self._ccols[value] = ap
        return ap


def chunk_cols(dc, w=None):
    if w is None:
        return slice(dc * NTOK, (dc + 1) * NTOK)
    return slice(dc * NTOK + w * CH, dc * NTOK + w * CH + CH)



def emit_load_fm(k, src_hbm, dst_sb, nrows, f32=True):
    blk = k.block
    sl = k.sem_unique("ldfm")

    @blk.sync
    def _(sy):
        for dc in range(nrows // 128):
            sy.dma_start(out=dst_sb[:, dc * NTOK:(dc + 1) * NTOK],
                         in_=src_hbm[dc * 128:(dc + 1) * 128, :]).then_inc(
                sl.h, 16)
            sl.inc(16)
    return sl.n


def emit_load_xT(k, x_in, xT):
    blk = k.block
    slp = [k.sem_unique("lxp"), k.sem_unique("lxp")]
    sp = k.sem("pe_a")
    sv = k.sem("dve")
    ph = contextlib.ExitStack()
    stage = [ph.enter_context(k.nc.sbuf_tensor(f"xstg{i}", [128, D], F32))
             for i in range(2)]
    ntt = NTOK // 128
    ngr = DCN // 4
    p0, v0 = sp.n, sv.n
    gd = k.guard_counts()

    @blk.sync
    def _(sy):
        k.guard(sy, gd)
        for i in range(ntt):
            if i >= 2:
                sy.wait_ge(sp.h, p0 + (i - 1) * ngr)
            sy.dma_start(out=stage[i % 2][:],
                         in_=x_in[i * 128:(i + 1) * 128, :]).then_inc(
                slp[i % 2].h, 16)
            slp[i % 2].inc(16)

    @blk.tensor
    def _(t):
        for i in range(ntt):
            t.wait_ge(slp[i % 2].h, 16 * (i // 2 + 1))
            for g in range(ngr):
                gi = i * ngr + g
                if gi >= 2:
                    t.wait_ge(sv.h, v0 + gi - 1)
                bank = k.ps_f[gi % 2]
                for j in range(4):
                    dc = g * 4 + j
                    m = t.transpose(bank[:, j * 128:(j + 1) * 128],
                                    stage[i % 2][:, dc * 128:(dc + 1) * 128],
                                    k.ident_f[:])
                m.then_inc(sp.h, 1)
                sp.inc(1)

    @blk.vector
    def _(v):
        v = _VecSync(k, v)
        for i in range(ntt):
            for g in range(ngr):
                gi = i * ngr + g
                v.wait_ge(sp.h, p0 + gi + 1)
                bank = k.ps_f[gi % 2]
                dst = xT[:].rearrange("p (c n) -> p c n", n=NTOK)
                v.tensor_copy(
                    out=dst[:, g * 4:(g + 1) * 4, i * 128:(i + 1) * 128],
                    in_=bank[:].rearrange("p (j f) -> p j f", f=128),
                )
    ph.close()
    return sv.n


def emit_store_fm(k, src_sb, out_hbm, nrows):
    blk = k.block
    st = k.sem_unique("stfm")
    k.track_store(st)
    gd = k.guard_counts()

    @blk.sync
    def _(sy):
        k.guard(sy, gd)
        for dc in range(nrows // 128):
            sy.dma_start(out=out_hbm[dc * 128:(dc + 1) * 128, :],
                         in_=src_sb[:, dc * NTOK:(dc + 1) * NTOK]).then_inc(
                st.h, 16)
            st.inc(16)
    return st.n


def emit_ln(k, xT, yT, want_csum, csum_sb, tag):
    blk = k.block
    sa = k.sem("act")
    sp = k.sem("pe_a")
    sv = k.sem("dve")
    lsc = contextlib.ExitStack()
    EL = lsc.enter_context
    xb = [EL(k.nc.sbuf_tensor(f"lnxb{tag}{i}", [128, CH], BF16))
          for i in range(2)]
    x2 = [EL(k.nc.sbuf_tensor(f"lnx2{tag}{i}", [128, CH], BF16))
          for i in range(2)]
    rows = EL(k.nc.sbuf_tensor(f"lnrows{tag}", [1, 4 * CH], F32))
    bc = EL(k.nc.sbuf_tensor(f"lnbc{tag}", [128, 2 * CH], F32))
    gd = k.guard_counts()

    for w in range(NW):
        a0, p0 = sa.n, sp.n

        @blk.scalar
        def _(s, w=w, a0=a0, p0=p0):
            if w == 0:
                k.guard(s, gd)
            for dc in range(DCN):
                if dc >= 2:
                    s.wait_ge(sp.h, p0 + dc - 1)
                sl_ = xT[:, chunk_cols(dc, w)]
                s.activation(out=xb[dc % 2][:], in_=sl_, func=AF.Copy)
                s.activation(out=x2[dc % 2][:], in_=sl_,
                             func=AF.Square).then_inc(sa.h, 1)
                sa.inc(1)

        @blk.tensor
        def _(t, a0=a0, w=w):
            if w == 0:
                k.guard(t, gd)
            for dc in range(DCN):
                t.wait_ge(sa.h, a0 + dc + 1)
                t.matmul(k.ps_f[0][0:1, :], k.ones_col_bf[:], xb[dc % 2][:],
                         start=(dc == 0), stop=(dc == DCN - 1))
                t.matmul(k.ps_f[1][0:1, :], k.ones_col_bf[:], x2[dc % 2][:],
                         start=(dc == 0),
                         stop=(dc == DCN - 1)).then_inc(sp.h, 1)
                sp.inc(1)
        sums_done = sp.n

        S1 = rows[0:1, 0 * CH:1 * CH]
        S2 = rows[0:1, 1 * CH:2 * CH]
        NMR = rows[0:1, 2 * CH:3 * CH]
        RST = rows[0:1, 3 * CH:4 * CH]

        @blk.vector
        def _(v, sums_done=sums_done):
            v = _VecSync(k, v)
            v.wait_ge(sp.h, sums_done)
            v.tensor_scalar(out=S1, in0=k.ps_f[0][0:1, :], scalar1=1.0 / D,
                            scalar2=None, op0=ALU.mult)
            v.tensor_scalar(out=S2, in0=k.ps_f[1][0:1, :], scalar1=1.0 / D,
                            scalar2=None, op0=ALU.mult)
            v.tensor_tensor(out=NMR, in0=S1, in1=S1, op=ALU.mult)
            v.tensor_tensor(out=S2, in0=S2, in1=NMR, op=ALU.subtract)
            v.tensor_scalar(out=S2, in0=S2, scalar1=1e-5, scalar2=None,
                            op0=ALU.add)
        var_done = sv.n

        @blk.scalar
        def _(s, var_done=var_done):
            s.wait_ge(sv.h, var_done)
            s.activation(out=RST, in_=S2, func=AF.Sqrt).then_inc(sa.h, 1)
            sa.inc(1)
        sq_done = sa.n

        @blk.vector
        def _(v, sq_done=sq_done):
            v = _VecSync(k, v)
            v.wait_ge(sa.h, sq_done)
            v.reciprocal(out=RST, in_=RST)
            v.tensor_tensor(out=NMR, in0=S1, in1=RST, op=ALU.mult)
            v.tensor_scalar(out=NMR, in0=NMR, scalar1=-1.0, scalar2=None,
                            op0=ALU.mult)
        rows_done = sv.n

        @blk.tensor
        def _(t, rows_done=rows_done):
            t.wait_ge(sv.h, rows_done)
            t.matmul(k.ps_f[2][:], k.ones_row_f[:], NMR, start=True, stop=True)
            t.matmul(k.ps_f[3][:], k.ones_row_f[:], RST, start=True,
                     stop=True).then_inc(sp.h, 1)
            sp.inc(1)
        bc_done = sp.n

        @blk.scalar
        def _(s, bc_done=bc_done):
            s.wait_ge(sp.h, bc_done)
            s.activation(out=bc[:, 0:CH], in_=k.ps_f[2][:], func=AF.Copy)
            s.activation(out=bc[:, CH:2 * CH], in_=k.ps_f[3][:],
                         func=AF.Copy).then_inc(sa.h, 1)
            sa.inc(1)
        bcs_done = sa.n

        @blk.vector
        def _(v, w=w, bcs_done=bcs_done):
            v = _VecSync(k, v)
            v.wait_ge(sa.h, bcs_done)
            for dc in range(DCN):
                v.tensor_tensor(out=k.scr512[:], in0=xT[:, chunk_cols(dc, w)],
                                in1=bc[:, 0:CH], op=ALU.add)
                v.tensor_tensor(out=yT[:, chunk_cols(dc, w)], in0=k.scr512[:],
                                in1=bc[:, CH:2 * CH], op=ALU.mult)
        norm_done = sv.n

        if want_csum:
            @blk.tensor
            def _(t, w=w, norm_done=norm_done):
                t.wait_ge(sv.h, norm_done)
                for dc in range(DCN):
                    m = t.matmul(k.ps_f[4][0:1, :], k.ones_col_bf[:],
                                 yT[:, chunk_cols(dc, w)],
                                 start=(dc == 0), stop=(dc == DCN - 1))
                m.then_inc(sp.h, 1)
                sp.inc(1)
            cs_done = sp.n

            @blk.vector
            def _(v, w=w, cs_done=cs_done):
                v = _VecSync(k, v)
                v.wait_ge(sp.h, cs_done)
                v.tensor_copy(out=csum_sb[0:1, w * CH:(w + 1) * CH],
                              in_=k.ps_f[4][0:1, :])
    lsc.close()
    return k.sem("dve").n


def emit_vec_cols(k, src_hbm, n, dst_sb, scale=None):
    blk = k.block
    sl = k.sem_unique("ldvec")
    sv = k.sem("dve")
    nch = n // 128

    @blk.sync
    def _(sy):
        for c in range(nch):
            sy.dma_start(out=dst_sb[:, c:c + 1],
                         in_=src_hbm[c * 128:(c + 1) * 128].rearrange(
                             "(p o) -> p o", o=1)).then_inc(sl.h, 16)
            sl.inc(16)
    ld_done = sl.n

    @blk.vector
    def _(v):
        v = _VecSync(k, v)
        v.wait_ge(sl.h, ld_done)
        if scale is not None:
            v.tensor_scalar(out=dst_sb[:], in0=dst_sb[:], scalar1=scale,
                            scalar2=None, op0=ALU.mult)
        else:
            v.tensor_copy(out=dst_sb[:], in_=dst_sb[:])
    return sv.n


def emit_ffn0(k, yT, csum_sb, xT, wq_gate, wq_up, wq_down, sg_sb, su_sb,
              halfsd_sb, tag):
    blk = k.block
    sgp = [k.sem_unique("f0g"), k.sem_unique("f0g")]
    sdp = [k.sem_unique("f0d"), k.sem_unique("f0d")]
    stp = k.sem("pe_b")
    ste = k.sem("act_b")
    smm = k.sem("pe_a")
    sae = k.sem("act")
    shm = k.sem("dve")

    with contextlib.ExitStack() as ph:
        E = ph.enter_context
        k.ps_b = [E(k.nc.psum_tensor(f"f0psb{tag}{i}", [128, 1024], BF16))
                  for i in range(2)]
        h_sb = E(k.nc.sbuf_tensor(f"f0h{tag}", [128, HC0 * CH], BF16))
        ustg = [E(k.nc.sbuf_tensor(f"f0u{tag}{i}", [128, CH], BF16))
                for i in range(4)]
        hs_row = E(k.nc.sbuf_tensor(f"f0hr{tag}", [1, CH], F32))

        for w in range(NW):
            s1sc = contextlib.ExitStack()
            E1 = s1sc.enter_context
            wg_stage = [E1(k.nc.sbuf_tensor(f"f0wg{tag}{w}{i}", [128, D], BF16))
                        for i in range(2)]
            wu_stage = [E1(k.nc.sbuf_tensor(f"f0wu{tag}{w}{i}", [128, D], BF16))
                        for i in range(2)]
            wgT = [E1(k.nc.sbuf_tensor(f"f0wgT{tag}{w}{i}", [128, D], BF16))
                   for i in range(2)]
            wuT = [E1(k.nc.sbuf_tensor(f"f0wuT{tag}{w}{i}", [128, D], BF16))
                   for i in range(2)]
            gb0 = [sgp[0].n, sgp[1].n]
            tp0, te0, mm0, ae0, hm0 = stp.n, ste.n, smm.n, sae.n, shm.n
            gd1 = k.guard_counts()

            @blk.gpsimd
            def _(g, tp0=tp0):
                k.guard(g, gd1)
                for hc in range(HC0):
                    if hc >= 2:
                        g.wait_ge(stp.h, tp0 + 4 * (hc - 1))
                    g.dma_start(out=wg_stage[hc % 2][:],
                                in_=wq_gate[hc * 128:(hc + 1) * 128, :]
                                ).then_inc(sgp[hc % 2].h, 16)
                    g.dma_start(out=wu_stage[hc % 2][:],
                                in_=wq_up[hc * 128:(hc + 1) * 128, :]
                                ).then_inc(sgp[hc % 2].h, 16)
                    sgp[hc % 2].inc(32)

            @blk.tensor
            def _(t, w=w, gb0=gb0, te0=te0, ae0=ae0):
                for hc in range(HC0):
                    t.wait_ge(sgp[hc % 2].h, gb0[hc % 2] + 32 * (hc // 2 + 1))
                    for half, stage in enumerate((wg_stage, wu_stage)):
                        for grp in range(2):
                            gi = hc * 4 + half * 2 + grp
                            if gi >= 2:
                                t.wait_ge(ste.h, te0 + gi - 1)
                            bank = k.ps_b[gi % 2]
                            for j in range(8):
                                dcc = grp * 8 + j
                                m = t.transpose(
                                    bank[:, j * 128:(j + 1) * 128],
                                    stage[hc % 2][:, dcc * 128:(dcc + 1) * 128],
                                    k.ident_b[:])
                            m.then_inc(stp.h, 1)
                            stp.inc(1)
                    t.wait_ge(ste.h, te0 + hc * 4 + 4)
                    if hc >= 2:
                        t.wait_ge(sae.h, ae0 + 2 * (hc - 1))
                    gb = k.ps_f[hc % 2]
                    ub = k.ps_f[2 + hc % 2]
                    for dc in range(DCN):
                        t.matmul(gb[:], wgT[hc % 2][:, dc * 128:(dc + 1) * 128],
                                 yT[:, chunk_cols(dc, w)],
                                 start=(dc == 0), stop=False)
                        t.matmul(ub[:], wuT[hc % 2][:, dc * 128:(dc + 1) * 128],
                                 yT[:, chunk_cols(dc, w)],
                                 start=(dc == 0), stop=False)
                    t.matmul(gb[:], k.neg128_row[:],
                             csum_sb[0:1, w * CH:(w + 1) * CH],
                             start=False, stop=True)
                    t.matmul(ub[:], k.neg128_row[:],
                             csum_sb[0:1, w * CH:(w + 1) * CH],
                             start=False, stop=True).then_inc(smm.h, 1)
                    smm.inc(1)

            @blk.scalar
            def _(s, w=w, tp0=tp0, mm0=mm0, hm0=hm0):
                for hc in range(HC0 + 1):
                    if hc < HC0:
                        for half, dstT in enumerate((wgT, wuT)):
                            for grp in range(2):
                                gi = hc * 4 + half * 2 + grp
                                s.wait_ge(stp.h, tp0 + gi + 1)
                                s.activation(
                                    out=dstT[hc % 2][:,
                                             grp * 1024:(grp + 1) * 1024],
                                    in_=k.ps_b[gi % 2][:],
                                    func=AF.Copy).then_inc(ste.h, 1)
                                ste.inc(1)
                    if hc >= 1:
                        hp = hc - 1
                        s.wait_ge(smm.h, mm0 + hp + 1)
                        if hp >= 4:
                            s.wait_ge(shm.h, hm0 + hp - 3)
                        s.activation(out=h_sb[:, hp * CH:(hp + 1) * CH],
                                     in_=k.ps_f[hp % 2][:], func=AF.Silu,
                                     scale=sg_sb[:, hp:hp + 1])
                        s.mul(out=ustg[hp % 4][:], in_=k.ps_f[2 + hp % 2][:],
                              mul=su_sb[:, hp:hp + 1]).then_inc(sae.h, 2)
                        sae.inc(2)

            @blk.vector
            def _(v, w=w, ae0=ae0):
                v = _VecSync(k, v)
                for hc in range(HC0):
                    v.wait_ge(sae.h, ae0 + 2 * (hc + 1))
                    v.tensor_tensor(out=h_sb[:, hc * CH:(hc + 1) * CH],
                                    in0=h_sb[:, hc * CH:(hc + 1) * CH],
                                    in1=ustg[hc % 4][:], op=ALU.mult)
            h_done = shm.n

            @blk.tensor
            def _(t, h_done=h_done):
                t.wait_ge(shm.h, h_done)
                for hc in range(HC0):
                    m = t.matmul(k.ps_f[4][0:1, :], k.ones_col_bf[:],
                                 h_sb[:, hc * CH:(hc + 1) * CH],
                                 start=(hc == 0), stop=(hc == HC0 - 1))
                m.then_inc(smm.h, 1)
                smm.inc(1)
            hs_done = smm.n

            @blk.vector
            def _(v, hs_done=hs_done):
                v = _VecSync(k, v)
                v.wait_ge(smm.h, hs_done)
                v.tensor_copy(out=hs_row[0:1, :], in_=k.ps_f[4][0:1, :])
            hsr_done = shm.n

            NGR = (HC0 + 7) // 8
            s1sc.close()
            s3sc = contextlib.ExitStack()
            E3 = s3sc.enter_context
            wdn_stage = [E3(k.nc.sbuf_tensor(f"f0wd{tag}{w}{i}", [128, HID0],
                                             BF16)) for i in range(2)]
            wdT = E3(k.nc.sbuf_tensor(f"f0wdT{tag}{w}", [128, HID0], BF16))
            db0 = [sdp[0].n, sdp[1].n]
            tp1, te1, mm1, hm1 = stp.n, ste.n, smm.n, shm.n
            gd3 = k.guard_counts()

            @blk.gpsimd
            def _(g, tp1=tp1):
                k.guard(g, gd3)
                for dc in range(DCN):
                    if dc >= 2:
                        g.wait_ge(stp.h, tp1 + NGR * (dc - 1))
                    g.dma_start(out=wdn_stage[dc % 2][:],
                                in_=wq_down[dc * 128:(dc + 1) * 128, :]
                                ).then_inc(sdp[dc % 2].h, 16)
                    sdp[dc % 2].inc(16)

            @blk.tensor
            def _(t, w=w, db0=db0, te1=te1, hm1=hm1, hsr_done=hsr_done):
                t.wait_ge(shm.h, hsr_done)
                for dc in range(DCN):
                    t.wait_ge(sdp[dc % 2].h, db0[dc % 2] + 16 * (dc // 2 + 1))
                    for grp in range(NGR):
                        gi = dc * NGR + grp
                        if gi >= 2:
                            t.wait_ge(ste.h, te1 + gi - 1)
                        nblk = min(8, HC0 - grp * 8)
                        bank = k.ps_b[gi % 2]
                        for j in range(nblk):
                            hcc = grp * 8 + j
                            m = t.transpose(
                                bank[:, j * 128:(j + 1) * 128],
                                wdn_stage[dc % 2][:, hcc * 128:(hcc + 1) * 128],
                                k.ident_b[:])
                        m.then_inc(stp.h, 1)
                        stp.inc(1)
                    t.wait_ge(ste.h, te1 + (dc + 1) * NGR)
                    if dc >= 2:
                        t.wait_ge(shm.h, hm1 + dc - 1)
                    bank = k.ps_f[dc % 2]
                    for hc in range(HC0):
                        t.matmul(bank[:],
                                 wdT[:, hc * 128:(hc + 1) * 128],
                                 h_sb[:, hc * CH:(hc + 1) * CH],
                                 start=(hc == 0), stop=False)
                    t.matmul(bank[:], k.neg128_row[:], hs_row[0:1, :],
                             start=False, stop=True).then_inc(smm.h, 1)
                    smm.inc(1)

            @blk.scalar
            def _(s, tp1=tp1, mm1=mm1):
                for dc in range(DCN):
                    for grp in range(NGR):
                        gi = dc * NGR + grp
                        s.wait_ge(stp.h, tp1 + gi + 1)
                        if dc >= 1 and grp == 0:
                            s.wait_ge(smm.h, mm1 + dc)
                        nblk = min(8, HC0 - grp * 8)
                        s.activation(
                            out=wdT[:, grp * 1024:grp * 1024 + nblk * 128],
                            in_=k.ps_b[gi % 2][:, 0:nblk * 128],
                            func=AF.Copy).then_inc(ste.h, 1)
                        ste.inc(1)

            @blk.vector
            def _(v, w=w, mm1=mm1):
                v = _VecSync(k, v)
                for dc in range(DCN):
                    v.wait_ge(smm.h, mm1 + dc + 1)
                    v.scalar_tensor_tensor(
                        out=xT[:, chunk_cols(dc, w)],
                        in0=k.ps_f[dc % 2][:],
                        scalar=halfsd_sb[:, dc:dc + 1],
                        in1=xT[:, chunk_cols(dc, w)],
                        op0=ALU.mult, op1=ALU.add)
            s3sc.close()
    return k.sem("dve").n


def emit_proj(k, yT, w_hbm, dout_n, out_hbm, scale, tag, out_bf=True):
    blk = k.block
    sg = k.sem_unique("pjw")
    smm = k.sem("pe_a")
    sae = k.sem("act")
    stp_ = [k.sem_unique("pst"), k.sem_unique("pst")]
    k.track_store(stp_[0])
    k.track_store(stp_[1])
    mcn = dout_n // 128

    with contextlib.ExitStack() as ph:
        E = ph.enter_context
        w_sb = [E(k.nc.sbuf_tensor(f"pj{tag}w{i}", [128, dout_n], BF16))
                for i in range(DCN)]
        ostg = [E(k.nc.sbuf_tensor(f"pj{tag}o{i}", [128, NTOK], BF16))
                for i in range(2)]
        scol = k.constcol(scale) if scale != 1.0 else None
        g0, mm0, ae0 = sg.n, smm.n, sae.n
        gd = k.guard_counts()

        @blk.gpsimd
        def _(g):
            k.guard(g, gd)
            for kc in range(DCN):
                g.dma_start(out=w_sb[kc][:],
                            in_=w_hbm[kc * 128:(kc + 1) * 128, :]).then_inc(
                    sg.h, 16)
                sg.inc(16)

        @blk.tensor
        def _(t, g0=g0, ae0=ae0):
            t.wait_ge(sg.h, g0 + 16 * DCN)
            for mc in range(mcn):
                for w in range(NW):
                    i = mc * NW + w
                    if i >= 4:
                        t.wait_ge(sae.h, ae0 + i - 3)
                    bank = k.ps_f[i % 4]
                    for kc in range(DCN):
                        m = t.matmul(bank[:],
                                     w_sb[kc][:, mc * 128:(mc + 1) * 128],
                                     yT[:, chunk_cols(kc, w)],
                                     start=(kc == 0), stop=(kc == DCN - 1))
                    m.then_inc(smm.h, 1)
                    smm.inc(1)

        @blk.scalar
        def _(s, mm0=mm0):
            k.guard(s, gd)
            for mc in range(mcn):
                for w in range(NW):
                    i = mc * NW + w
                    s.wait_ge(smm.h, mm0 + i + 1)
                    if mc >= 2 and w == 0:
                        s.wait_ge(stp_[mc % 2].h, 16 * (mc // 2))
                    if scol is not None:
                        s.activation(out=ostg[mc % 2][:, w * CH:(w + 1) * CH],
                                     in_=k.ps_f[i % 4][:], func=AF.Copy,
                                     scale=scol).then_inc(sae.h, 1)
                    else:
                        s.activation(out=ostg[mc % 2][:, w * CH:(w + 1) * CH],
                                     in_=k.ps_f[i % 4][:],
                                     func=AF.Copy).then_inc(sae.h, 1)
                    sae.inc(1)

        @blk.sync
        def _(sy, ae0=ae0):
            for mc in range(mcn):
                sy.wait_ge(sae.h, ae0 + NW * (mc + 1))
                sy.dma_start(out=out_hbm[mc * 128:(mc + 1) * 128, :],
                             in_=ostg[mc % 2][:]).then_inc(stp_[mc % 2].h, 16)
                stp_[mc % 2].inc(16)
    return None


def emit_outproj(k, ctx_sb, w, Wo_in, xT, bout_col=None, tag=""):
    blk = k.block
    sg = k.sem_unique("opw")
    smm = k.sem("pe_a")
    sv = k.sem("dve")
    with contextlib.ExitStack() as ph:
        E = ph.enter_context
        w_sb = [E(k.nc.sbuf_tensor(f"op{tag}w{i}", [128, D], BF16))
                for i in range(DCN)]
        gd = k.guard_counts()
        g0, mm0 = sg.n, smm.n

        @blk.gpsimd
        def _(g):
            k.guard(g, gd)
            for kc in range(DCN):
                g.dma_start(out=w_sb[kc][:],
                            in_=Wo_in[kc * 128:(kc + 1) * 128, :]).then_inc(
                    sg.h, 16)
                sg.inc(16)

        dv = sv.n
        evac_mark = [dv + mc + 1 for mc in range(DCN)]

        @blk.tensor
        def _(t, g0=g0):
            k.guard(t, gd)
            t.wait_ge(sg.h, g0 + 16 * DCN)
            for mc in range(DCN):
                if mc >= 4:
                    t.wait_ge(sv.h, evac_mark[mc - 4])
                bank = k.ps_f[mc % 4]
                for kc in range(DCN):
                    m = t.matmul(bank[:],
                                 w_sb[kc][:, mc * 128:(mc + 1) * 128],
                                 ctx_sb[:, kc * CH:(kc + 1) * CH],
                                 start=(kc == 0), stop=(kc == DCN - 1))
                m.then_inc(smm.h, 1)
                smm.inc(1)

        @blk.vector
        def _(v, mm0=mm0):
            v = _VecSync(k, v)
            for mc in range(DCN):
                v.wait_ge(smm.h, mm0 + mc + 1)
                sc = bout_col[:, mc:mc + 1] if bout_col is not None else 0.0
                v.scalar_tensor_tensor(
                    out=xT[:, chunk_cols(mc, w)], in0=k.ps_f[mc % 4][:],
                    scalar=sc, in1=xT[:, chunk_cols(mc, w)],
                    op0=ALU.add, op1=ALU.add)
        assert k.sem("dve").n == evac_mark[-1]
    return k.sem("dve").n


def emit_attn0(k, qT_in, KW, VW, Vfull, wgt_in, dcorr_in, flags_in, Wo_in,
               md_sb, xT):
    NH, HD, NKV = 16, 128, 4
    blk = k.block
    spb = k.sem("pe_b")
    spa = k.sem("pe_a")
    sa = k.sem("act")
    sv = k.sem("dve")
    E50F = float(np.exp(np.float64(-50.0)))
    E50P = float(np.exp(np.float64(50.0)))
    NSLOT = [8, 16]

    with contextlib.ExitStack() as ph:
        E = ph.enter_context
        suff_sb = E(k.nc.sbuf_tensor("a0suffc", [128, 2 * NKV], F32))
        dcorr_sb = E(k.nc.sbuf_tensor("a0dc", [1, 2], F32))
        flags_sb = E(k.nc.sbuf_tensor("a0fl", [128, 24], BF16))

        sfx = contextlib.ExitStack()
        ES = sfx.enter_context
        scr_rows = ES(k.nc.sbuf_tensor("a0srow", [1, 2 * CH], F32))
        wgt_sb = ES(k.nc.sbuf_tensor("a0wg", [1, 2 * NCH], F32))
        gd0 = k.guard_counts()
        vst = [ES(k.nc.sbuf_tensor(f"a0vst{i}", [128, CH], BF16))
               for i in range(2)]
        bsum = ES(k.nc.sbuf_tensor("a0bs", [1, NCH * CH], F32))
        slp = [k.sem_unique("a0v"), k.sem_unique("a0v")]
        sld = k.sem_unique("a0ld")
        p0, v0 = spb.n, sv.n

        @blk.sync
        def _(sy):
            k.guard(sy, gd0)
            sy.dma_start(out=dcorr_sb[:],
                         in_=dcorr_in).then_inc(sld.h, 16)
            sy.dma_start(out=wgt_sb[:], in_=wgt_in).then_inc(sld.h, 16)
            sy.dma_start(out=flags_sb[:], in_=flags_in).then_inc(sld.h, 16)
            sld.inc(48)
            for i in range(4 * NCH):
                if i >= 2:
                    sy.wait_ge(spb.h, p0 + i - 1)
                sy.dma_start(out=vst[i % 2][:],
                             in_=Vfull[i * 128:(i + 1) * 128, :]).then_inc(
                    slp[i % 2].h, 16)
                slp[i % 2].inc(16)
        ld_done = sld.n

        @blk.tensor
        def _(t):
            k.guard(t, gd0)
            for i in range(4 * NCH):
                g, t4 = divmod(i, 4)
                t.wait_ge(slp[i % 2].h, 16 * (i // 2 + 1))
                if t4 == 0 and g >= 2:
                    t.wait_ge(sv.h, v0 + g - 1)
                m = t.matmul(k.ps_f[4 + g % 2][0:1, :], k.ones_col_bf[:],
                             vst[i % 2][:], start=(t4 == 0), stop=(t4 == 3))
                m.then_inc(spb.h, 1)
                spb.inc(1)
        sums_done = spb.n

        @blk.vector
        def _(v):
            v = _VecSync(k, v)
            for g in range(NCH):
                v.wait_ge(spb.h, p0 + 4 * (g + 1))
                v.tensor_copy(out=bsum[0:1, g * CH:(g + 1) * CH],
                              in_=k.ps_f[4 + g % 2][0:1, :])
            v.wait_ge(sld.h, ld_done)
            for w in range(2):
                v.memset(scr_rows[0:1, w * CH:(w + 1) * CH], 0.0)
                for g in range(NCH):
                    v.scalar_tensor_tensor(
                        out=scr_rows[0:1, w * CH:(w + 1) * CH],
                        in0=bsum[0:1, g * CH:(g + 1) * CH],
                        scalar=wgt_sb[0:1, w * NCH + g:w * NCH + g + 1],
                        in1=scr_rows[0:1, w * CH:(w + 1) * CH],
                        op0=ALU.mult, op1=ALU.add)
        rows_ev = sv.n
        tr0 = spb.n

        @blk.tensor
        def _(t):
            t.wait_ge(sv.h, rows_ev)
            for kh in range(NKV):
                t.transpose(k.ps_f[0][:, 0:1],
                            scr_rows[0:1, kh * 128:(kh + 1) * 128],
                            k.ident_f[0:1, 0:1])
                m = t.transpose(k.ps_f[0][:, 1:2],
                                scr_rows[0:1, CH + kh * 128:
                                         CH + (kh + 1) * 128],
                                k.ident_f[0:1, 0:1])
                m.then_inc(spb.h, 1)
                spb.inc(1)
                t.wait_ge(sv.h, rows_ev + 2 * (kh + 1))

        @blk.vector
        def _(v):
            v = _VecSync(k, v)
            for kh in range(NKV):
                v.wait_ge(spb.h, tr0 + kh + 1)
                v.tensor_copy(out=suff_sb[:, kh:kh + 1],
                              in_=k.ps_f[0][:, 0:1])
                v.tensor_copy(out=suff_sb[:, NKV + kh:NKV + kh + 1],
                              in_=k.ps_f[0][:, 1:2])
        suff_done = sv.n
        sfx.close()

        for w in range(2):
            nslot = NSLOT[w]
            slots = list(range(nslot))
            segs = [slots[i:i + 8] for i in range(0, nslot, 8)]
            K_in = KW[w]
            V_in = VW[w]
            csc = contextlib.ExitStack()
            ctx_sb = csc.enter_context(
                k.nc.sbuf_tensor(f"a0c{w}", [128, NH * CH], BF16))
            wsc = contextlib.ExitStack()
            EW = wsc.enter_context
            kseg = EW(k.nc.sbuf_tensor(f"a0k{w}", [128, NKV * 8 * CH], BF16))
            vseg = EW(k.nc.sbuf_tensor(f"a0vs{w}", [128, 8 * 4 * CH], BF16))
            qst = [EW(k.nc.sbuf_tensor(f"a0q{w}{i}", [128, CH], BF16))
                   for i in range(2)]
            D_sb = EW(k.nc.sbuf_tensor(f"a0D{w}", [1, NH * CH], F32))
            pt = [EW(k.nc.sbuf_tensor(f"a0p{w}{i}", [128, CH], BF16))
                  for i in range(2)]
            bcb = EW(k.nc.sbuf_tensor(f"a0bc{w}", [128, 2 * CH], F32))
            gdw = k.guard_counts()

            ctx_bank_free = [None, None]
            bc_slot_free = [None, None]
            prev_seg_end = [None]

            for seg_i, seg in enumerate(segs):
                last_seg = seg_i == len(segs) - 1
                nkb = 4 * len(seg)
                useg = k.sem_unique("a0s")

                pb, pa, ac, dv = spb.n, spa.n, sa.n, sv.n
                scores_mark, spa_mark, exp_mark = {}, {}, {}
                bcastpe_mark, bc_mark, clamp_mark = {}, {}, {}
                ctxevac_mark, recip_mark, ctxn_mark = {}, {}, {}
                for h in range(NH):
                    for i in range(nkb):
                        pb += 1
                        scores_mark[(h, i)] = pb
                    if last_seg:
                        pb += 1
                        bcastpe_mark[h] = pb
                    for i in range(nkb):
                        pa += 1
                        spa_mark[(h, i)] = pa
                    for i in range(nkb):
                        ac += 1
                        exp_mark[(h, i)] = ac
                    if last_seg:
                        ac += 1
                        bc_mark[h] = ac
                    for i in range(nkb):
                        dv += 1
                        clamp_mark[(h, i)] = dv
                    dv += 2
                    ctxevac_mark[h] = dv
                    if last_seg:
                        dv += 2
                        recip_mark[h] = dv
                        dv += 2
                        ctxn_mark[h] = dv
                seg_end_counts = (pb, pa)
                sb_free = [None, None]
                uqp = [k.sem_unique("a0qp"), k.sem_unique("a0qp")]

                @blk.sync
                def _(sy, seg=seg, seg_i=seg_i, useg=useg, w=w):
                    if seg_i == 0 and prev_seg_end[0] is None:
                        k.guard(sy, gdw)
                    if prev_seg_end[0] is not None:
                        sy.wait_ge(spb.h, prev_seg_end[0][0])
                        sy.wait_ge(spa.h, prev_seg_end[0][1])
                    for p, b in enumerate(seg):
                        for kh in range(NKV):
                            sy.dma_start(
                                out=kseg[:, (kh * 8 + p) * CH:
                                         (kh * 8 + p + 1) * CH],
                                in_=K_in[kh * 128:(kh + 1) * 128,
                                         b * CH:(b + 1) * CH]).then_inc(
                                useg.h, 16)
                            useg.inc(16)
                        for t_ in range(4):
                            row = b * 4 + t_
                            sy.dma_start(
                                out=vseg[:, (p * 4 + t_) * CH:
                                         (p * 4 + t_ + 1) * CH],
                                in_=V_in[row * 128:(row + 1) * 128, :]
                            ).then_inc(useg.h, 16)
                            useg.inc(16)
                    for h in range(NH):
                        if h >= 2:
                            sy.wait_ge(spb.h, scores_mark[(h - 2, nkb - 1)])
                        sy.dma_start(
                            out=qst[h % 2][:],
                            in_=qT_in[h * 128:(h + 1) * 128,
                                      w * CH:(w + 1) * CH]).then_inc(
                            uqp[h % 2].h, 16)
                        uqp[h % 2].inc(16)
                seg_ld = useg.n

                @blk.tensor
                def _(t, seg=seg, seg_i=seg_i, nkb=nkb, w=w, nslot=nslot,
                      seg_ld=seg_ld, useg=useg, last_seg=last_seg,
                      scores_mark=scores_mark, spa_mark=spa_mark,
                      exp_mark=exp_mark, clamp_mark=clamp_mark,
                      bc_mark=bc_mark, ctxevac_mark=ctxevac_mark,
                      recip_mark=recip_mark):
                    if seg_i == 0 and w == 0:
                        k.guard(t, gdw)
                    t.wait_ge(useg.h, seg_ld)
                    if seg_i == 0 and w == 0:
                        t.wait_ge(sv.h, suff_done)
                    for h in range(NH):
                        kh = h // 4
                        t.wait_ge(uqp[h % 2].h, 16 * (h // 2 + 1))

                        def scores(i, h=h, kh=kh):
                            p, ks = divmod(i, 4)
                            bank = k.ps_f[4 + i % 2]
                            if sb_free[i % 2] is not None:
                                t.wait_ge(sa.h, sb_free[i % 2][1])
                            diag = (seg[p] == nslot - 1)
                            m = t.matmul(
                                bank[:],
                                kseg[:, (kh * 8 + p) * CH + ks * 128:
                                     (kh * 8 + p) * CH + ks * 128 + 128],
                                qst[h % 2][:],
                                start=True, stop=not diag)
                            if diag:
                                m = t.matmul(
                                    bank[:], k.ident_b[:],
                                    md_sb[:, ks * CH:(ks + 1) * CH],
                                    start=False, stop=True)
                            m.then_inc(spb.h, 1)
                            spb.inc(1)
                            sb_free[i % 2] = ("exp", exp_mark[(h, i)])

                        scores(0)
                        scores(1)
                        for i in range(nkb):
                            t.wait_ge(sv.h, clamp_mark[(h, i)])
                            if i == 0 and ctx_bank_free[h % 2] is not None:
                                t.wait_ge(sv.h, ctx_bank_free[h % 2])
                            p, ks = divmod(i, 4)
                            t.matmul(k.ps_f[h % 2][:],
                                     vseg[:, (p * 4 + ks) * CH + kh * 128:
                                          (p * 4 + ks) * CH + kh * 128 + 128],
                                     pt[i % 2][:],
                                     start=(i == 0), stop=(i == nkb - 1))
                            fcol = flags_sb[:, 8 * w + seg[p]:
                                            8 * w + seg[p] + 1]
                            m = t.matmul(k.ps_f[2 + h % 2][0:1, :],
                                         fcol, pt[i % 2][:],
                                         start=(i == 0), stop=(i == nkb - 1))
                            m.then_inc(spa.h, 1)
                            spa.inc(1)
                            if i + 2 < nkb:
                                scores(i + 2)
                        if last_seg:
                            t.wait_ge(sv.h, recip_mark[h])
                            bank = k.ps_f[4 + nkb % 2]
                            if sb_free[nkb % 2] is not None:
                                t.wait_ge(sa.h, sb_free[nkb % 2][1])
                            m = t.matmul(
                                bank[:], k.ones_row_f[0:1, 0:128],
                                D_sb[0:1, h * CH:(h + 1) * CH],
                                start=True, stop=True)
                            m.then_inc(spb.h, 1)
                            spb.inc(1)
                            sb_free[nkb % 2] = ("bc", bc_mark[h])
                        ctx_bank_free[h % 2] = ctxevac_mark[h]

                @blk.scalar
                def _(s, seg_i=seg_i, nkb=nkb, last_seg=last_seg,
                      scores_mark=scores_mark, spa_mark=spa_mark,
                      bcastpe_mark=bcastpe_mark, ctxn_mark=ctxn_mark):
                    for h in range(NH):
                        for i in range(nkb):
                            s.wait_ge(spb.h, scores_mark[(h, i)])
                            if (h, i - 2) in spa_mark:
                                s.wait_ge(spa.h, spa_mark[(h, i - 2)])
                            elif h >= 1 and i < 2:
                                s.wait_ge(spa.h,
                                          spa_mark[(h - 1, nkb - 2 + i)])
                            s.activation(out=pt[i % 2][:],
                                         in_=k.ps_f[4 + i % 2][:],
                                         func=AF.Exp).then_inc(sa.h, 1)
                            sa.inc(1)
                        if last_seg:
                            s.wait_ge(spb.h, bcastpe_mark[h])
                            if bc_slot_free[h % 2] is not None:
                                s.wait_ge(sv.h, bc_slot_free[h % 2])
                            s.activation(
                                out=bcb[:, (h % 2) * CH:(h % 2 + 1) * CH],
                                in_=k.ps_f[4 + nkb % 2][:],
                                func=AF.Copy).then_inc(sa.h, 1)
                            sa.inc(1)
                            bc_slot_free[h % 2] = ctxn_mark[h]

                @blk.vector
                def _(v, seg_i=seg_i, nkb=nkb, w=w, last_seg=last_seg,
                      exp_mark=exp_mark, spa_mark=spa_mark, bc_mark=bc_mark,
                      ctxevac_mark=ctxevac_mark, recip_mark=recip_mark,
                      ctxn_mark=ctxn_mark):
                    v = _VecSync(k, v)
                    for h in range(NH):
                        kh = h // 4
                        for i in range(nkb):
                            v.wait_ge(sa.h, exp_mark[(h, i)])
                            v.tensor_scalar(out=pt[i % 2][:],
                                            in0=pt[i % 2][:],
                                            scalar1=E50F, scalar2=E50P,
                                            op0=ALU.max, op1=ALU.min)
                        v.wait_ge(spa.h, spa_mark[(h, nkb - 1)])
                        if seg_i == 0:
                            v.tensor_copy(out=ctx_sb[:, h * CH:(h + 1) * CH],
                                          in_=k.ps_f[h % 2][:])
                            v.tensor_copy(out=D_sb[0:1, h * CH:(h + 1) * CH],
                                          in_=k.ps_f[2 + h % 2][0:1, :])
                        else:
                            v.tensor_tensor(out=ctx_sb[:, h * CH:(h + 1) * CH],
                                            in0=k.ps_f[h % 2][:],
                                            in1=ctx_sb[:, h * CH:(h + 1) * CH],
                                            op=ALU.add)
                            v.tensor_tensor(
                                out=D_sb[0:1, h * CH:(h + 1) * CH],
                                in0=k.ps_f[2 + h % 2][0:1, :],
                                in1=D_sb[0:1, h * CH:(h + 1) * CH],
                                op=ALU.add)
                        assert k.sem("dve").n == ctxevac_mark[h]
                        if last_seg:
                            v.tensor_scalar(
                                out=D_sb[0:1, h * CH:(h + 1) * CH],
                                in0=D_sb[0:1, h * CH:(h + 1) * CH],
                                scalar1=dcorr_sb[0:1, w:w + 1], scalar2=None,
                                op0=ALU.add)
                            v.reciprocal(out=D_sb[0:1, h * CH:(h + 1) * CH],
                                         in_=D_sb[0:1, h * CH:(h + 1) * CH])
                            assert k.sem("dve").n == recip_mark[h]
                            v.wait_ge(sa.h, bc_mark[h])
                            v.tensor_scalar(
                                out=k.scr512[:],
                                in0=ctx_sb[:, h * CH:(h + 1) * CH],
                                scalar1=suff_sb[:, w * NKV + kh:
                                                w * NKV + kh + 1],
                                scalar2=None, op0=ALU.add)
                            v.tensor_tensor(
                                out=ctx_sb[:, h * CH:(h + 1) * CH],
                                in0=k.scr512[:],
                                in1=bcb[:, (h % 2) * CH:(h % 2 + 1) * CH],
                                op=ALU.mult)
                            assert k.sem("dve").n == ctxn_mark[h]
                prev_seg_end[0] = seg_end_counts

            wsc.close()
            emit_outproj(k, ctx_sb, w, Wo_in, xT, None, tag=f"a0{w}")
            csc.close()
    return k.sem("dve").n


def emit_ffn1(k, yT, xT, wg_in, wu_in, wd_in, tag):
    blk = k.block
    smm = k.sem("pe_a")
    sae = k.sem("act")
    sv = k.sem("dve")
    NIW = HID1 // 512

    with contextlib.ExitStack() as ph:
        E = ph.enter_context
        ps_x = [E(k.nc.psum_tensor(f"f1px{tag}{i}", [128, 512], F32))
                for i in range(2)]
        psG = [k.ps_f[0], k.ps_f[1], k.ps_f[2], k.ps_f[3]]
        psU = [k.ps_f[4], k.ps_f[5], ps_x[0], ps_x[1]]
        h_sb = E(k.nc.sbuf_tensor(f"f1h{tag}", [128, HC1 * CH], BF16))

        for w in range(NW):
            gsc = contextlib.ExitStack()
            EG = gsc.enter_context
            wbuf = [[EG(k.nc.sbuf_tensor(f"f1w{tag}{w}{m}{g}",
                                         [128, 4 * CH], BF16))
                     for g in range(2)] for m in range(2)]
            gstg = [EG(k.nc.sbuf_tensor(f"f1g{tag}{w}{i}", [128, CH], BF16))
                    for i in range(2)]
            ustg = [EG(k.nc.sbuf_tensor(f"f1u{tag}{w}{i}", [128, CH], BF16))
                    for i in range(2)]
            sq = [EG(k.nc.sbuf_tensor(f"f1sq{tag}{w}{i}", [128, CH], F32))
                  for i in range(4)]
            pr = [EG(k.nc.sbuf_tensor(f"f1pr{tag}{w}{i}", [128, CH], F32))
                  for i in range(3)]
            geb = EG(k.nc.sbuf_tensor(f"f1ge{tag}{w}", [128, CH], BF16))
            gd1 = k.guard_counts()
            wlp = [k.sem_unique("f1w"), k.sem_unique("f1w")]
            mm0, ae0, dv0 = smm.n, sae.n, sv.n

            def pe_mark(iw, q, mm0=mm0):
                return mm0 + iw * 4 + q + 1

            def act_mark(iw, j8, ae0=ae0):
                return ae0 + iw * 8 + j8 + 1

            @blk.gpsimd
            def _(g, w=w, mm0=mm0):
                k.guard(g, gd1)
                for it in range(NIW * 4):
                    iw, q = divmod(it, 4)
                    if it >= 2:
                        piw, pq = divmod(it - 2, 4)
                        g.wait_ge(smm.h, pe_mark(piw, pq))
                    for mat, src in enumerate((wg_in, wu_in)):
                        for dci in range(4):
                            dc = q * 4 + dci
                            g.dma_start(
                                out=wbuf[mat][it % 2][:,
                                                      dci * CH:(dci + 1) * CH],
                                in_=src[dc * 128:(dc + 1) * 128,
                                        iw * CH:(iw + 1) * CH]).then_inc(
                                wlp[it % 2].h, 16)
                            wlp[it % 2].inc(16)

            @blk.tensor
            def _(t, w=w, mm0=mm0, ae0=ae0):
                k.guard(t, gd1)
                for iw in range(NIW):
                    for q in range(4):
                        it = iw * 4 + q
                        t.wait_ge(wlp[it % 2].h, 128 * (it // 2 + 1))
                        if iw >= 1 and q == 0:
                            t.wait_ge(sae.h, act_mark(iw - 1, 7))
                        for dci in range(4):
                            dc = q * 4 + dci
                            for j in range(4):
                                lhsG = wbuf[0][it % 2][:,
                                                       dci * CH:(dci + 1) * CH]
                                lhsU = wbuf[1][it % 2][:,
                                                       dci * CH:(dci + 1) * CH]
                                lG = lhsG.rearrange("p (g f) -> p g f",
                                                    f=4)[:, :, j]
                                lU = lhsU.rearrange("p (g f) -> p g f",
                                                    f=4)[:, :, j]
                                st_ = (q == 0 and dci == 0)
                                sp_ = (q == 3 and dci == 3)
                                t.matmul(psG[j][:], lG,
                                         yT[:, chunk_cols(dc, w)],
                                         start=st_, stop=sp_)
                                m = t.matmul(psU[j][:], lU,
                                             yT[:, chunk_cols(dc, w)],
                                             start=st_, stop=sp_)
                        m.then_inc(smm.h, 1)
                        smm.inc(1)

            @blk.scalar
            def _(s, w=w, mm0=mm0, dv0=dv0):
                for iw in range(NIW):
                    s.wait_ge(smm.h, pe_mark(iw, 3))
                    for j in range(4):
                        if j >= 2:
                            s.wait_ge(sv.h, dv0 + iw * 23 + j - 1)
                        elif iw >= 1:
                            s.wait_ge(sv.h, dv0 + (iw - 1) * 23 + j + 3)
                        s.activation(out=gstg[j % 2][:], in_=psG[j][:],
                                     func=AF.Silu).then_inc(sae.h, 1)
                        sae.inc(1)
                        s.mul(out=ustg[j % 2][:], in_=psU[j][:],
                              mul=1.0).then_inc(sae.h, 1)
                        sae.inc(1)

            @blk.vector
            def _(v, w=w, ae0=ae0, dv0=dv0):
                v = _VecSync(k, v)
                for iw in range(NIW):
                    for j in range(4):
                        c = 4 * iw + j
                        v.wait_ge(sae.h, act_mark(iw, j * 2 + 1))
                        v.tensor_tensor(out=h_sb[:, c * CH:(c + 1) * CH],
                                        in0=gstg[j % 2][:], in1=ustg[j % 2][:],
                                        op=ALU.mult)
                    c0 = 4 * iw
                    hs = [h_sb[:, (c0 + j) * CH:(c0 + j + 1) * CH]
                          for j in range(4)]
                    for j in range(4):
                        v.tensor_tensor(out=sq[j][:], in0=hs[j], in1=hs[j],
                                        op=ALU.mult)
                    v.tensor_tensor(out=pr[0][:], in0=sq[0][:], in1=sq[1][:],
                                    op=ALU.max)
                    v.tensor_tensor(out=pr[1][:], in0=sq[2][:], in1=sq[3][:],
                                    op=ALU.max)
                    v.tensor_tensor(out=pr[0][:], in0=pr[0][:], in1=pr[1][:],
                                    op=ALU.min)
                    v.tensor_tensor(out=pr[1][:], in0=sq[0][:], in1=sq[1][:],
                                    op=ALU.min)
                    v.tensor_tensor(out=pr[2][:], in0=sq[2][:], in1=sq[3][:],
                                    op=ALU.min)
                    v.tensor_tensor(out=pr[1][:], in0=pr[1][:], in1=pr[2][:],
                                    op=ALU.max)
                    v.tensor_tensor(out=pr[0][:], in0=pr[0][:], in1=pr[1][:],
                                    op=ALU.max)
                    for j in range(4):
                        v.tensor_tensor(out=geb[:], in0=sq[j][:], in1=pr[0][:],
                                        op=ALU.is_ge)
                        v.tensor_tensor(out=hs[j], in0=hs[j], in1=geb[:],
                                        op=ALU.mult)
                assert k.sem("dve").n == dv0 + NIW * 23
            h_done = sv.n
            gsc.close()

            dsc = contextlib.ExitStack()
            wdi = [dsc.enter_context(
                k.nc.sbuf_tensor(f"f1wd{tag}{w}{i}", [128, HC1 * 128], BF16))
                for i in range(2)]
            gd3 = k.guard_counts()
            dlp = [k.sem_unique("f1d"), k.sem_unique("f1d")]
            mm1, dv1 = smm.n, sv.n
            wd4 = wd_in.rearrange("(cq pp cm) d -> cq cm pp d", cm=4, pp=128)

            @blk.gpsimd
            def _(g, w=w, mm1=mm1):
                k.guard(g, gd3)
                for dc in range(DCN):
                    if dc >= 2:
                        g.wait_ge(smm.h, mm1 + dc - 1)
                    dst4 = wdi[dc % 2][:].rearrange(
                        "p (cq cm f) -> p cq cm f", cq=16, cm=4)
                    for cm in range(4):
                        g.dma_start(
                            out=dst4[:, :, cm, :],
                            in_=wd4[:, cm, :,
                                    dc * 128:(dc + 1) * 128].rearrange(
                                "cq pp f -> pp cq f")).then_inc(
                            dlp[dc % 2].h, 16)
                        dlp[dc % 2].inc(16)

            dve_evac = [dv1 + dc + 1 for dc in range(DCN)]

            @blk.tensor
            def _(t, w=w, mm1=mm1, h_done=h_done):
                t.wait_ge(sv.h, h_done)
                for dc in range(DCN):
                    t.wait_ge(dlp[dc % 2].h, 64 * (dc // 2 + 1))
                    if dc >= 2:
                        t.wait_ge(sv.h, dve_evac[dc - 2])
                    bank = ps_x[dc % 2]
                    for c in range(HC1):
                        m = t.matmul(bank[:],
                                     wdi[dc % 2][:, c * 128:(c + 1) * 128],
                                     h_sb[:, c * CH:(c + 1) * CH],
                                     start=(c == 0), stop=(c == HC1 - 1))
                    m.then_inc(smm.h, 1)
                    smm.inc(1)

            @blk.vector
            def _(v, w=w, mm1=mm1):
                v = _VecSync(k, v)
                for dc in range(DCN):
                    v.wait_ge(smm.h, mm1 + dc + 1)
                    v.scalar_tensor_tensor(
                        out=xT[:, chunk_cols(dc, w)], in0=ps_x[dc % 2][:],
                        scalar=0.5, in1=xT[:, chunk_cols(dc, w)],
                        op0=ALU.mult, op1=ALU.add)
                assert k.sem("dve").n == dve_evac[-1]
            dsc.close()
    return k.sem("dve").n


def emit_rope_proj(k, yT, w_hbm, dout_n, out_hbm, scale, Ctab, Stab, tag):
    blk = k.block
    sg = k.sem_unique("rpw")
    smm = k.sem("pe_a")
    sae = k.sem("act")
    sv = k.sem("dve")
    stp_ = [k.sem_unique("rst"), k.sem_unique("rst")]
    k.track_store(stp_[0])
    k.track_store(stp_[1])
    mcn = dout_n // 128

    with contextlib.ExitStack() as ph:
        E = ph.enter_context
        w_sb = [E(k.nc.sbuf_tensor(f"rp{tag}w{i}", [128, dout_n], BF16))
                for i in range(DCN)]
        qf = E(k.nc.sbuf_tensor(f"rp{tag}qf", [128, NTOK], F32))
        qsw = E(k.nc.sbuf_tensor(f"rp{tag}qs", [128, NTOK], F32))
        qt1 = E(k.nc.sbuf_tensor(f"rp{tag}t1", [128, NTOK], F32))
        ostg = [E(k.nc.sbuf_tensor(f"rp{tag}o{i}", [128, NTOK], BF16))
                for i in range(2)]
        scol = k.constcol(scale) if scale != 1.0 else None
        gd = k.guard_counts()
        g0, mm0, ae0, dv0 = sg.n, smm.n, sae.n, sv.n

        def act_mark(mc, w, ae0=ae0):
            return ae0 + mc * 2 + w + 1

        def dve_mark(mc, dv0=dv0):
            return dv0 + (mc + 1) * 7

        @blk.gpsimd
        def _(g):
            k.guard(g, gd)
            for kc in range(DCN):
                g.dma_start(out=w_sb[kc][:],
                            in_=w_hbm[kc * 128:(kc + 1) * 128, :]).then_inc(
                    sg.h, 16)
                sg.inc(16)

        @blk.tensor
        def _(t, g0=g0):
            k.guard(t, gd)
            t.wait_ge(sg.h, g0 + 16 * DCN)
            for mc in range(mcn):
                for w in range(NW):
                    i = mc * NW + w
                    if i >= 4:
                        pmc, pw = divmod(i - 4, NW)
                        t.wait_ge(sae.h, act_mark(pmc, pw))
                    bank = k.ps_f[i % 4]
                    for kc in range(DCN):
                        m = t.matmul(bank[:],
                                     w_sb[kc][:, mc * 128:(mc + 1) * 128],
                                     yT[:, chunk_cols(kc, w)],
                                     start=(kc == 0), stop=(kc == DCN - 1))
                    m.then_inc(smm.h, 1)
                    smm.inc(1)

        @blk.scalar
        def _(s, mm0=mm0):
            k.guard(s, gd)
            for mc in range(mcn):
                if mc >= 1:
                    s.wait_ge(sv.h, dve_mark(mc - 1) - 2)
                for w in range(NW):
                    i = mc * NW + w
                    s.wait_ge(smm.h, mm0 + i + 1)
                    if scol is not None:
                        s.activation(out=qf[:, w * CH:(w + 1) * CH],
                                     in_=k.ps_f[i % 4][:], func=AF.Copy,
                                     scale=scol).then_inc(sae.h, 1)
                    else:
                        s.activation(out=qf[:, w * CH:(w + 1) * CH],
                                     in_=k.ps_f[i % 4][:],
                                     func=AF.Copy).then_inc(sae.h, 1)
                    sae.inc(1)

        @blk.vector
        def _(v, ae0=ae0):
            v = _VecSync(k, v)
            for mc in range(mcn):
                v.wait_ge(sae.h, act_mark(mc, NW - 1))
                if mc >= 2:
                    v.wait_ge(stp_[mc % 2].h, 16 * (mc // 2))
                v.tensor_copy(out=qsw[0:32, :], in_=qf[32:64, :])
                v.tensor_copy(out=qsw[32:64, :], in_=qf[0:32, :])
                v.tensor_copy(out=qsw[64:96, :], in_=qf[96:128, :])
                v.tensor_copy(out=qsw[96:128, :], in_=qf[64:96, :])
                v.tensor_tensor(out=qt1[:], in0=qf[:], in1=Ctab[:],
                                op=ALU.mult)
                v.tensor_tensor(out=qsw[:], in0=qsw[:], in1=Stab[:],
                                op=ALU.mult)
                v.tensor_tensor(out=ostg[mc % 2][:], in0=qt1[:], in1=qsw[:],
                                op=ALU.add)
                assert k.sem("dve").n == dve_mark(mc)

        @blk.sync
        def _(sy, dv0=dv0):
            for mc in range(mcn):
                sy.wait_ge(sv.h, dve_mark(mc))
                sy.dma_start(out=out_hbm[mc * 128:(mc + 1) * 128, :],
                             in_=ostg[mc % 2][:]).then_inc(stp_[mc % 2].h, 16)
                stp_[mc % 2].inc(16)
    return k.sem("dve").n


def emit_attn1(k, qT_in, KW, VW, dcorr_in, flags_in, Wo_in, md_sb,
               bout_col, xT):
    NH, HD, NKV = 32, 64, 8
    blk = k.block
    spb = k.sem("pe_b")
    spa = k.sem("pe_a")
    sa = k.sem("act")
    sv = k.sem("dve")
    SEGB = 4
    NSLOT = [8, 16]

    def qslot(h):
        return (h // 8) * 4 + (h % 4)

    def qbase(h):
        return 64 * ((h // 4) % 2)

    with contextlib.ExitStack() as ph:
        E = ph.enter_context
        scr_row = E(k.nc.sbuf_tensor("a1srow", [1, CH], F32))
        dcorr_sb = E(k.nc.sbuf_tensor("a1dc", [1, 2], F32))
        flags_sb = E(k.nc.sbuf_tensor("a1fl", [128, 24], BF16))
        sld = k.sem_unique("a1ld")
        gdd = k.guard_counts()

        @blk.sync
        def _(sy):
            k.guard(sy, gdd)
            sy.dma_start(out=dcorr_sb[:], in_=dcorr_in).then_inc(sld.h, 16)
            sy.dma_start(out=flags_sb[:], in_=flags_in).then_inc(sld.h, 16)
            sld.inc(32)
        dc_done = sld.n

        for w in range(2):
            nslot = NSLOT[w]
            slots = list(range(nslot))
            segs = [slots[i:i + SEGB] for i in range(0, nslot, SEGB)]
            K_in = KW[w]
            V_in = VW[w]
            csc = contextlib.ExitStack()
            ctx_sb = csc.enter_context(
                k.nc.sbuf_tensor(f"a1c{w}", [128, 16 * CH], BF16))
            wsc = contextlib.ExitStack()
            EW = wsc.enter_context
            kseg = EW(k.nc.sbuf_tensor(f"a1k{w}", [128, 4 * SEGB * CH], BF16))
            vseg = EW(k.nc.sbuf_tensor(f"a1vs{w}", [128, SEGB * 4 * CH], BF16))
            qst = [EW(k.nc.sbuf_tensor(f"a1q{w}{i}", [128, CH], BF16))
                   for i in range(2)]
            D_sb = EW(k.nc.sbuf_tensor(f"a1D{w}", [1, NH * CH], F32))
            pt = [EW(k.nc.sbuf_tensor(f"a1p{w}{i}", [128, CH], BF16))
                  for i in range(2)]
            bcb = EW(k.nc.sbuf_tensor(f"a1bc{w}", [128, 2 * CH], F32))
            gdw = k.guard_counts()

            ctx_bank_free = [None, None]
            bc_slot_free = [None, None]
            prev_seg_end = [None]

            for seg_i, seg in enumerate(segs):
                last_seg = seg_i == len(segs) - 1
                nkb = 4 * len(seg)
                useg = k.sem_unique("a1s")

                pb, pa, ac, dv = spb.n, spa.n, sa.n, sv.n
                scores_mark, spa_mark, exp_mark = {}, {}, {}
                bcastpe_mark, bc_mark = {}, {}
                ctxevac_mark, recip_mark, ctxn_mark = {}, {}, {}
                for h in range(NH):
                    for i in range(nkb):
                        pb += 1
                        scores_mark[(h, i)] = pb
                    if last_seg:
                        pb += 1
                        bcastpe_mark[h] = pb
                    for i in range(nkb):
                        pa += 1
                        spa_mark[(h, i)] = pa
                    for i in range(nkb):
                        ac += 1
                        exp_mark[(h, i)] = ac
                    if last_seg:
                        ac += 1
                        bc_mark[h] = ac
                    dv += 2 if seg_i == 0 else 3
                    ctxevac_mark[h] = dv
                    if last_seg:
                        dv += 2
                        recip_mark[h] = dv
                        dv += 1
                        ctxn_mark[h] = dv
                seg_end_counts = (pb, pa)
                sb_free = [None, None]

                uqp = [k.sem_unique("a1qp"), k.sem_unique("a1qp")]

                @blk.sync
                def _(sy, seg=seg, seg_i=seg_i, useg=useg, w=w):
                    if seg_i == 0 and prev_seg_end[0] is None:
                        k.guard(sy, gdw)
                    if prev_seg_end[0] is not None:
                        sy.wait_ge(spb.h, prev_seg_end[0][0])
                        sy.wait_ge(spa.h, prev_seg_end[0][1])
                    for p, b in enumerate(seg):
                        for pc in range(4):
                            sy.dma_start(
                                out=kseg[:, (pc * SEGB + p) * CH:
                                         (pc * SEGB + p + 1) * CH],
                                in_=K_in[pc * 128:(pc + 1) * 128,
                                         b * CH:(b + 1) * CH]).then_inc(
                                useg.h, 16)
                            useg.inc(16)
                        for t_ in range(4):
                            row = b * 4 + t_
                            sy.dma_start(
                                out=vseg[:, (p * 4 + t_) * CH:
                                         (p * 4 + t_ + 1) * CH],
                                in_=V_in[row * 128:(row + 1) * 128, :]
                            ).then_inc(useg.h, 16)
                            useg.inc(16)
                    for h in range(NH):
                        if h >= 2:
                            sy.wait_ge(spb.h, scores_mark[(h - 2, nkb - 1)])
                        sy.dma_start(
                            out=qst[h % 2][qbase(h):qbase(h) + 64, :],
                            in_=qT_in[h * 64:(h + 1) * 64,
                                      w * CH:(w + 1) * CH]).then_inc(
                            uqp[h % 2].h, 16)
                        uqp[h % 2].inc(16)
                seg_ld = useg.n

                @blk.tensor
                def _(t, seg=seg, seg_i=seg_i, nkb=nkb, w=w, nslot=nslot,
                      seg_ld=seg_ld, useg=useg, last_seg=last_seg,
                      scores_mark=scores_mark, spa_mark=spa_mark,
                      exp_mark=exp_mark, bcastpe_mark=bcastpe_mark,
                      bc_mark=bc_mark, ctxevac_mark=ctxevac_mark,
                      recip_mark=recip_mark, ctxn_mark=ctxn_mark):
                    if seg_i == 0 and w == 0:
                        k.guard(t, gdw)
                    t.wait_ge(useg.h, seg_ld)
                    for h in range(NH):
                        kh = h // 4
                        pc, khp = kh // 2, kh % 2
                        kb_base = 64 * khp
                        t.wait_ge(uqp[h % 2].h, 16 * (h // 2 + 1))

                        def scores(i, h=h, kh=kh, pc=pc, kb_base=kb_base):
                            p, ks = divmod(i, 4)
                            bank = k.ps_f[4 + i % 2]
                            if sb_free[i % 2] is not None:
                                t.wait_ge(sa.h, sb_free[i % 2][1])
                            diag = (seg[p] == nslot - 1)
                            m = t.matmul(
                                bank[:],
                                kseg[kb_base:kb_base + 64,
                                     (pc * SEGB + p) * CH + ks * 128:
                                     (pc * SEGB + p) * CH + ks * 128 + 128],
                                qst[h % 2][qbase(h):qbase(h) + 64, :],
                                start=True, stop=not diag,
                                tile_position=(kb_base, 0))
                            if diag:
                                m = t.matmul(
                                    bank[:], k.ident_b[:],
                                    md_sb[:, ks * CH:(ks + 1) * CH],
                                    start=False, stop=True)
                            m.then_inc(spb.h, 1)
                            spb.inc(1)
                            sb_free[i % 2] = ("exp", exp_mark[(h, i)])

                        scores(0)
                        scores(1)
                        for i in range(nkb):
                            t.wait_ge(sa.h, exp_mark[(h, i)])
                            if i == 0 and ctx_bank_free[h % 2] is not None:
                                t.wait_ge(sv.h, ctx_bank_free[h % 2])
                            p, ks = divmod(i, 4)
                            cbank = k.ps_f[h % 2]
                            hp = h % 2
                            t.matmul(cbank[64 * hp:64 * hp + 64, :],
                                     vseg[:, (p * 4 + ks) * CH + kh * 64:
                                          (p * 4 + ks) * CH + kh * 64 + 64],
                                     pt[i % 2][:],
                                     start=(i == 0), stop=(i == nkb - 1),
                                     tile_position=(0, 64 * hp),
                                     skip_group_check=True)
                            fcol = flags_sb[:, 8 * w + seg[p]:
                                            8 * w + seg[p] + 1]
                            m = t.matmul(
                                cbank[64 * (1 - hp):64 * (1 - hp) + 1, :],
                                fcol, pt[i % 2][:],
                                start=(i == 0), stop=(i == nkb - 1),
                                tile_position=(0, 64 * (1 - hp)),
                                skip_group_check=True)
                            m.then_inc(spa.h, 1)
                            spa.inc(1)
                            if i + 2 < nkb:
                                scores(i + 2)
                        if last_seg:
                            t.wait_ge(sv.h, recip_mark[h])
                            bank = k.ps_f[4 + nkb % 2]
                            if sb_free[nkb % 2] is not None:
                                t.wait_ge(sa.h, sb_free[nkb % 2][1])
                            m = t.matmul(
                                bank[:], k.ones_row_f[0:1, 0:128],
                                D_sb[0:1, h * CH:(h + 1) * CH],
                                start=True, stop=True)
                            m.then_inc(spb.h, 1)
                            spb.inc(1)
                            sb_free[nkb % 2] = ("bc", bc_mark[h])
                        ctx_bank_free[h % 2] = ctxevac_mark[h]

                @blk.scalar
                def _(s, seg_i=seg_i, nkb=nkb, last_seg=last_seg,
                      scores_mark=scores_mark, spa_mark=spa_mark,
                      bcastpe_mark=bcastpe_mark, ctxn_mark=ctxn_mark):
                    for h in range(NH):
                        for i in range(nkb):
                            s.wait_ge(spb.h, scores_mark[(h, i)])
                            if (h, i - 2) in spa_mark:
                                s.wait_ge(spa.h, spa_mark[(h, i - 2)])
                            elif h >= 1 and i < 2:
                                s.wait_ge(spa.h,
                                          spa_mark[(h - 1, nkb - 2 + i)])
                            s.activation(out=pt[i % 2][:],
                                         in_=k.ps_f[4 + i % 2][:],
                                         func=AF.Exp).then_inc(sa.h, 1)
                            sa.inc(1)
                        if last_seg:
                            s.wait_ge(spb.h, bcastpe_mark[h])
                            if bc_slot_free[h % 2] is not None:
                                s.wait_ge(sv.h, bc_slot_free[h % 2])
                            s.activation(
                                out=bcb[:, (h % 2) * CH:(h % 2 + 1) * CH],
                                in_=k.ps_f[4 + nkb % 2][:],
                                func=AF.Copy).then_inc(sa.h, 1)
                            sa.inc(1)
                            bc_slot_free[h % 2] = ctxn_mark[h]

                @blk.vector
                def _(v, seg_i=seg_i, nkb=nkb, w=w, last_seg=last_seg,
                      spa_mark=spa_mark, bc_mark=bc_mark,
                      ctxevac_mark=ctxevac_mark, recip_mark=recip_mark,
                      ctxn_mark=ctxn_mark):
                    v = _VecSync(k, v)
                    for h in range(NH):
                        hp = h % 2
                        crows = slice(64 * hp, 64 * hp + 64)
                        ccols = slice((h // 2) * CH, (h // 2 + 1) * CH)
                        v.wait_ge(spa.h, spa_mark[(h, nkb - 1)])
                        if seg_i == 0:
                            v.tensor_copy(out=ctx_sb[crows, ccols],
                                          in_=k.ps_f[hp][crows, :])
                            v.tensor_copy(
                                out=D_sb[0:1, h * CH:(h + 1) * CH],
                                in_=k.ps_f[hp][64 * (1 - hp):
                                               64 * (1 - hp) + 1, :])
                        else:
                            v.tensor_tensor(out=ctx_sb[crows, ccols],
                                            in0=k.ps_f[hp][crows, :],
                                            in1=ctx_sb[crows, ccols],
                                            op=ALU.add)
                            v.tensor_copy(out=scr_row[0:1, :],
                                          in_=k.ps_f[hp][64 * (1 - hp):
                                                         64 * (1 - hp) + 1, :])
                            v.tensor_tensor(
                                out=D_sb[0:1, h * CH:(h + 1) * CH],
                                in0=scr_row[0:1, :],
                                in1=D_sb[0:1, h * CH:(h + 1) * CH],
                                op=ALU.add)
                        assert k.sem("dve").n == ctxevac_mark[h]
                        if last_seg:
                            v.tensor_scalar(
                                out=D_sb[0:1, h * CH:(h + 1) * CH],
                                in0=D_sb[0:1, h * CH:(h + 1) * CH],
                                scalar1=dcorr_sb[0:1, w:w + 1], scalar2=None,
                                op0=ALU.add)
                            v.reciprocal(out=D_sb[0:1, h * CH:(h + 1) * CH],
                                         in_=D_sb[0:1, h * CH:(h + 1) * CH])
                            assert k.sem("dve").n == recip_mark[h]
                            v.wait_ge(sa.h, bc_mark[h])
                            bcs = bcb[:, (h % 2) * CH:(h % 2 + 1) * CH]
                            v.tensor_tensor(out=ctx_sb[crows, ccols],
                                            in0=ctx_sb[crows, ccols],
                                            in1=bcs[crows, :],
                                            op=ALU.mult)
                            assert k.sem("dve").n == ctxn_mark[h]
                prev_seg_end[0] = seg_end_counts

            wsc.close()
            emit_outproj(k, ctx_sb, w, Wo_in, xT, bout_col, tag=f"a1{w}")
            csc.close()
    return k.sem("dve").n


def emit_store_tok(k, xT, out_hbm):
    blk = k.block
    sp = k.sem("pe_b")
    sv = k.sem("dve")
    stp_ = [k.sem_unique("sttk"), k.sem_unique("sttk")]
    k.track_store(stp_[0])
    k.track_store(stp_[1])
    gd = k.guard_counts()
    ntt = NTOK // 128
    ngr = DCN // 4
    with contextlib.ExitStack() as ph:
        E = ph.enter_context
        ostg = [E(k.nc.sbuf_tensor(f"stk{i}", [128, D], F32))
                for i in range(2)]
        p0, v0 = sp.n, sv.n

        @blk.tensor
        def _(t):
            k.guard(t, gd)
            for i in range(ntt):
                for g in range(ngr):
                    gi = i * ngr + g
                    if gi >= 2:
                        t.wait_ge(sv.h, v0 + gi - 1)
                    bank = k.ps_f[gi % 2]
                    for jj in range(4):
                        dc = g * 4 + jj
                        m = t.transpose(
                            bank[:, jj * 128:(jj + 1) * 128],
                            xT[:, dc * NTOK + i * 128:
                               dc * NTOK + (i + 1) * 128],
                            k.ident_f[:])
                    m.then_inc(sp.h, 1)
                    sp.inc(1)

        @blk.vector
        def _(v):
            v = _VecSync(k, v)
            for i in range(ntt):
                if i >= 2:
                    v.wait_ge(stp_[i % 2].h, 16 * (i // 2))
                for g in range(ngr):
                    gi = i * ngr + g
                    v.wait_ge(sp.h, p0 + gi + 1)
                    v.tensor_copy(
                        out=ostg[i % 2][:, g * 512:(g + 1) * 512],
                        in_=k.ps_f[gi % 2][:])

        @blk.sync
        def _(sy):
            for i in range(ntt):
                sy.wait_ge(sv.h, v0 + (i + 1) * ngr)
                sy.dma_start(out=out_hbm[i * 128:(i + 1) * 128, :],
                             in_=ostg[i % 2][:]).then_inc(stp_[i % 2].h, 16)
                stp_[i % 2].inc(16)
    return None





def build_A():
    k = K("A")
    x_in = k.din("x_in", [NTOK, D], F32)
    wq_gate = k.din("wq_gate", [HID0, D], I32)
    wq_up = k.din("wq_up", [HID0, D], I32)
    wq_down = k.din("wq_down", [D, HID0], I32)
    s_gate = k.din("s_gate", [HID0], F32)
    s_up = k.din("s_up", [HID0], F32)
    s_down = k.din("s_down", [D], F32)
    Wq0 = k.din("Wq0", [D, D], F32)
    Wk0 = k.din("Wk0", [D, 512], F32)
    Wv0 = k.din("Wv0", [D, 512], F32)

    s1T_out = k.dout("s1T_out", [D, NTOK], F32)
    qT_out = k.dout("qT_out", [D, NTOK], BF16)
    kT_out = k.dout("kT_out", [512, NTOK], BF16)
    vT_out = k.dout("vT_out", [512, NTOK], BF16)

    k.setup_common()
    xT = k.sb("xT", [128, DCN * NTOK], F32)
    yT = k.sb("yT", [128, DCN * NTOK], BF16)
    csum = k.sb("csum", [1, NTOK], F32)
    sg_sb = k.sb("sg_sb", [128, HC0], F32)
    su_sb = k.sb("su_sb", [128, HC0], F32)
    halfsd = k.sb("halfsd", [128, DCN], F32)

    emit_vec_cols(k, s_gate, HID0, sg_sb)
    emit_vec_cols(k, s_up, HID0, su_sb)
    emit_vec_cols(k, s_down, D, halfsd, scale=0.5)
    emit_load_xT(k, x_in, xT)
    emit_ln(k, xT, yT, True, csum, "l1")
    emit_ffn0(k, yT, csum, xT, wq_gate, wq_up, wq_down, sg_sb, su_sb,
              halfsd, "a")
    emit_store_fm(k, xT, s1T_out, D)
    emit_ln(k, xT, yT, False, None, "l2")
    emit_proj(k, yT, Wq0, D, qT_out, SQ128, "q0")
    emit_proj(k, yT, Wk0, 512, kT_out, 1.0, "k0")
    emit_proj(k, yT, Wv0, 512, vT_out, 1.0, "v0")

    blk = k.block

    @blk.sync
    def _(sy):
        for s in k.store_sems:
            sy.wait_ge(s.h, s.n)

    k.es.close()
    return k.nc



def build_B(debug_stage=None):
    k = K("B")
    s1T_in = k.din("s1T_in", [D, NTOK], F32)
    qT_in = k.din("qT_in", [D, NTOK], BF16)
    KW0 = [k.din("KW0a", [512, 8 * CH], BF16),
           k.din("KW0b", [512, 16 * CH], BF16)]
    VW0 = [k.din("VW0a", [8 * CH, 512], BF16),
           k.din("VW0b", [16 * CH, 512], BF16)]
    V0full = k.din("V0full", [T, 512], BF16)
    wgt0 = k.din("wgt0", [1, 2 * NCH], F32)
    dcorr0 = k.din("dcorr0", [1, 2], F32)
    flags0 = k.din("flags0", [128, 24], BF16)
    Wo0 = k.din("Wo0", [D, D], F32)
    md0_in = k.din("md0_in", [128, 4 * CH], BF16)
    wq_gate = k.din("wq_gate", [HID0, D], I32)
    wq_up = k.din("wq_up", [HID0, D], I32)
    wq_down = k.din("wq_down", [D, HID0], I32)
    s_gate = k.din("s_gate", [HID0], F32)
    s_up = k.din("s_up", [HID0], F32)
    s_down = k.din("s_down", [D], F32)
    wg1 = k.din("wg1", [D, HID1], F32)
    wu1 = k.din("wu1", [D, HID1], F32)
    wd1 = k.din("wd1", [HID1, D], F32)
    Wq1 = k.din("Wq1", [D, D], F32)
    Wk1 = k.din("Wk1", [D, 512], F32)
    Wv1 = k.din("Wv1", [D, 512], F32)
    ropeC_in = k.din("ropeC_in", [128, NTOK], F32)
    ropeS_in = k.din("ropeS_in", [128, NTOK], F32)

    s4T_out = k.dout("s4T_out", [D, NTOK], F32)
    q1T_out = k.dout("q1T_out", [D, NTOK], BF16)
    k1T_out = k.dout("k1T_out", [512, NTOK], BF16)
    v1T_out = k.dout("v1T_out", [512, NTOK], BF16)

    k.setup_common()
    xT = k.sb("xT", [128, DCN * NTOK], F32)
    sg_sb = k.sb("sg_sb", [128, HC0], F32)
    su_sb = k.sb("su_sb", [128, HC0], F32)
    halfsd = k.sb("halfsd", [128, DCN], F32)

    emit_vec_cols(k, s_gate, HID0, sg_sb)
    emit_vec_cols(k, s_up, HID0, su_sb)
    emit_vec_cols(k, s_down, D, halfsd, scale=0.5)
    blk = k.block
    emit_load_fm(k, s1T_in, xT, D)
    with k.nc.sbuf_tensor("md0_sb", [128, 4 * CH], BF16) as md0_sb:
        slm = k.sem_unique("ldmisc")
        gdm = k.guard_counts()

        @blk.sync
        def _(sy):
            k.guard(sy, gdm)
            sy.dma_start(out=md0_sb[:], in_=md0_in).then_inc(slm.h, 16)
            slm.inc(16)
        misc_done = slm.n

        @blk.tensor
        def _(t):
            t.wait_ge(slm.h, misc_done)

        emit_attn0(k, qT_in, KW0, VW0, V0full, wgt0, dcorr0, flags0, Wo0,
                   md0_sb, xT)
    if debug_stage == "attn0":
        emit_store_fm(k, xT, s4T_out, D)

        @blk.sync
        def _(sy):
            for s in k.store_sems:
                sy.wait_ge(s.h, s.n)
        k.es.close()
        return k.nc
    with k.nc.sbuf_tensor("yTb1", [128, DCN * NTOK], BF16) as yT, \
            k.nc.sbuf_tensor("csum", [1, NTOK], F32) as csum:
        emit_ln(k, xT, yT, True, csum, "l3")
        emit_ffn0(k, yT, csum, xT, wq_gate, wq_up, wq_down, sg_sb, su_sb,
                  halfsd, "b")
    if debug_stage == "ffn0":
        emit_store_fm(k, xT, s4T_out, D)

        @blk.sync
        def _(sy):
            for s in k.store_sems:
                sy.wait_ge(s.h, s.n)
        k.es.close()
        return k.nc
    with k.nc.sbuf_tensor("yTb2", [128, DCN * NTOK], BF16) as yT:
        emit_ln(k, xT, yT, False, None, "l4")
        emit_ffn1(k, yT, xT, wg1, wu1, wd1, "b")
    emit_store_fm(k, xT, s4T_out, D)
    with k.nc.sbuf_tensor("yTb3", [128, DCN * NTOK], BF16) as yT, \
            k.nc.sbuf_tensor("ropeC", [128, NTOK], F32) as ropeC, \
            k.nc.sbuf_tensor("ropeS", [128, NTOK], F32) as ropeS:
        emit_ln(k, xT, yT, False, None, "l5")
        slr = k.sem_unique("ldrope")
        gdr = k.guard_counts()

        @blk.sync
        def _(sy):
            k.guard(sy, gdr)
            sy.dma_start(out=ropeC[:], in_=ropeC_in).then_inc(slr.h, 16)
            sy.dma_start(out=ropeS[:], in_=ropeS_in).then_inc(slr.h, 16)
            slr.inc(32)
        rope_done = slr.n

        @blk.vector
        def _(v):
            v.wait_ge(slr.h, rope_done)

        emit_rope_proj(k, yT, Wq1, D, q1T_out, 0.125, ropeC, ropeS, "q1")
        emit_rope_proj(k, yT, Wk1, 512, k1T_out, 1.0, ropeC, ropeS, "k1")
        emit_proj(k, yT, Wv1, 512, v1T_out, 1.0, "v1")

    @blk.sync
    def _(sy):
        for s in k.store_sems:
            sy.wait_ge(s.h, s.n)

    k.es.close()
    return k.nc


def build_C():
    k = K("C")
    s4T_in = k.din("s4T_in", [D, NTOK], F32)
    qT_in = k.din("q1T_in", [D, NTOK], BF16)
    KW1 = [k.din("KW1a", [512, 8 * CH], BF16),
           k.din("KW1b", [512, 16 * CH], BF16)]
    VW1 = [k.din("VW1a", [8 * CH, 512], BF16),
           k.din("VW1b", [16 * CH, 512], BF16)]
    dcorr1 = k.din("dcorr1", [1, 2], F32)
    flags1 = k.din("flags1", [128, 24], BF16)
    Wo1 = k.din("Wo1", [D, D], F32)
    b_out = k.din("b_out", [D], F32)
    md1_in = k.din("md1_in", [128, 4 * CH], BF16)
    wg1 = k.din("wg1", [D, HID1], F32)
    wu1 = k.din("wu1", [D, HID1], F32)
    wd1 = k.din("wd1", [HID1, D], F32)

    out_tok = k.dout("out_tok", [NTOK, D], F32)

    k.setup_common()
    xT = k.sb("xT", [128, DCN * NTOK], F32)
    bout_col = k.sb("bout_col", [128, DCN], F32)

    emit_vec_cols(k, b_out, D, bout_col)
    blk = k.block
    emit_load_fm(k, s4T_in, xT, D)
    with k.nc.sbuf_tensor("md1_sb", [128, 4 * CH], BF16) as md1_sb:
        slm = k.sem_unique("ldmisc")
        gdm = k.guard_counts()

        @blk.sync
        def _(sy):
            k.guard(sy, gdm)
            sy.dma_start(out=md1_sb[:], in_=md1_in).then_inc(slm.h, 16)
            slm.inc(16)
        misc_done = slm.n

        @blk.tensor
        def _(t):
            t.wait_ge(slm.h, misc_done)

        emit_attn1(k, qT_in, KW1, VW1, dcorr1, flags1, Wo1, md1_sb,
                   bout_col, xT)
    with k.nc.sbuf_tensor("yTc1", [128, DCN * NTOK], BF16) as yT:
        emit_ln(k, xT, yT, False, None, "l6")
        emit_ffn1(k, yT, xT, wg1, wu1, wd1, "c")
    emit_store_tok(k, xT, out_tok)

    @blk.sync
    def _(sy):
        for s in k.store_sems:
            sy.wait_ge(s.h, s.n)

    k.es.close()
    return k.nc



_IDENT_F = np.eye(128, dtype=np.float32)
_IDENT_B = np.eye(128, dtype=np.float32).astype(BF)
E50_64 = np.exp(np.float64(-50.0))


def _mask_tiles(neg):
    m = np.zeros((128, 4 * CH), np.float32)
    for ks in range(4):
        i = np.arange(128)[:, None] + 128 * ks
        col = np.arange(CH)[None, :]
        m[:, ks * CH:(ks + 1) * CH] = np.where(i > col, neg, 0.0)
    return m.astype(BF)


def _rope_tables(c):
    a, b = own_chunks(c)
    pos = np.concatenate([np.arange(a * CH, (a + 1) * CH),
                          np.arange(b * CH, (b + 1) * CH)]).astype(np.float64)
    e = np.arange(32, dtype=np.float64)
    inv = 1.0 / (10000.0 ** (e / 32.0))
    fr = np.outer(inv, pos)
    cos = np.cos(fr).astype(np.float32)
    sin = np.sin(fr).astype(np.float32)
    C = np.tile(cos, (4, 1))
    S = np.concatenate([-sin, sin, -sin, sin], axis=0)
    return np.ascontiguousarray(C), np.ascontiguousarray(S)


def _kv_slots(kT_full, v_full, j, nslot):
    Kw = np.zeros((512, nslot * CH), BF)
    Vw = np.zeros((nslot * CH, 512), BF)
    for s in range(j):
        Kw[:, s * CH:(s + 1) * CH] = kT_full[:, s * CH:(s + 1) * CH]
        Vw[s * CH:(s + 1) * CH] = v_full[s * CH:(s + 1) * CH]
    Kw[:, (nslot - 1) * CH:] = kT_full[:, j * CH:(j + 1) * CH]
    Vw[(nslot - 1) * CH:] = v_full[j * CH:(j + 1) * CH]
    return Kw, Vw


def _gather_kv(res, kkey, vkey):
    kT_full = np.zeros((512, T), BF)
    v_full = np.zeros((T, 512), BF)
    for c in range(NC):
        a, b = own_chunks(c)
        kc = res[c][kkey]
        vc = res[c][vkey]
        kT_full[:, a * CH:(a + 1) * CH] = kc[:, :CH]
        kT_full[:, b * CH:(b + 1) * CH] = kc[:, CH:]
        v_full[a * CH:(a + 1) * CH] = vc[:, :CH].T
        v_full[b * CH:(b + 1) * CH] = vc[:, CH:].T
    return np.ascontiguousarray(kT_full), np.ascontiguousarray(v_full)


def _run_A(inputs, shards):
    nc = build_A()
    in_maps = []
    for c in range(NC):
        in_maps.append({
            "x_in": shards[c],
            "wq_gate": inputs["L0_ffn_wq_gate"],
            "wq_up": inputs["L0_ffn_wq_up"],
            "wq_down": inputs["L0_ffn_wq_down"],
            "s_gate": inputs["L0_ffn_s_gate"],
            "s_up": inputs["L0_ffn_s_up"],
            "s_down": inputs["L0_ffn_s_down"],
            "Wq0": inputs["L0_attn_Wq"],
            "Wk0": inputs["L0_attn_Wk"],
            "Wv0": inputs["L0_attn_Wv"],
            "ident_f_in": _IDENT_F,
            "ident_b_in": _IDENT_B,
        })
    res = run_bass_kernel_spmd(nc, in_maps, core_ids=list(range(NC)))
    return res.results


_MD0 = _mask_tiles(-1e9)
_MD1 = _mask_tiles(-1e30)


def _run_B(inputs, resA, kT0, v0):
    nc = build_B()
    in_maps = []
    for c in range(NC):
        j1, j2 = own_chunks(c)
        K0a, V0a = _kv_slots(kT0, v0, j1, 8)
        K0b, V0b = _kv_slots(kT0, v0, j2, 16)
        dcorr = np.zeros((1, 2), np.float32)
        wgt = np.zeros((1, 2 * NCH), np.float32)
        flags = np.zeros((128, 24), np.float32)
        for w, (j, nslot) in enumerate(((j1, 8), (j2, 16))):
            beyond = T - CH * (j + 1)
            dcorr[0, w] = np.float32(beyond * E50_64)
            wgt[0, w * NCH + j + 1: w * NCH + NCH] = np.float32(E50_64)
            for s in range(j):
                flags[:, 8 * w + s] = 1.0
            flags[:, 8 * w + nslot - 1] = 1.0
        in_maps.append({
            "s1T_in": resA[c]["s1T_out"],
            "qT_in": resA[c]["qT_out"],
            "KW0a": K0a, "KW0b": K0b, "VW0a": V0a, "VW0b": V0b,
            "V0full": v0,
            "wgt0": wgt, "dcorr0": dcorr, "flags0": flags.astype(BF),
            "Wo0": inputs["L0_attn_Wo"],
            "md0_in": _MD0,
            "wq_gate": inputs["L0_ffn_wq_gate"],
            "wq_up": inputs["L0_ffn_wq_up"],
            "wq_down": inputs["L0_ffn_wq_down"],
            "s_gate": inputs["L0_ffn_s_gate"],
            "s_up": inputs["L0_ffn_s_up"],
            "s_down": inputs["L0_ffn_s_down"],
            "wg1": inputs["L1_ffn_w_gate"],
            "wu1": inputs["L1_ffn_w_up"],
            "wd1": inputs["L1_ffn_w_down"],
            "Wq1": inputs["L1_attn_Wq"],
            "Wk1": inputs["L1_attn_Wk"],
            "Wv1": inputs["L1_attn_Wv"],
            "ropeC_in": _ROPE_C[c], "ropeS_in": _ROPE_S[c],
            "ident_f_in": _IDENT_F,
            "ident_b_in": _IDENT_B,
        })
    res = run_bass_kernel_spmd(nc, in_maps, core_ids=list(range(NC)))
    return res.results


def _run_C(inputs, resB, kT1, v1):
    nc = build_C()
    in_maps = []
    for c in range(NC):
        j1, j2 = own_chunks(c)
        K1a, V1a = _kv_slots(kT1, v1, j1, 8)
        K1b, V1b = _kv_slots(kT1, v1, j2, 16)
        dcorr = np.zeros((1, 2), np.float32)
        flags = np.zeros((128, 24), np.float32)
        for w, (j, nslot) in enumerate(((j1, 8), (j2, 16))):
            for s in range(j):
                flags[:, 8 * w + s] = 1.0
            flags[:, 8 * w + nslot - 1] = 1.0
        in_maps.append({
            "s4T_in": resB[c]["s4T_out"],
            "q1T_in": resB[c]["q1T_out"],
            "KW1a": K1a, "KW1b": K1b, "VW1a": V1a, "VW1b": V1b,
            "dcorr1": dcorr, "flags1": flags.astype(BF),
            "Wo1": inputs["L1_attn_Wo"],
            "b_out": inputs["L1_attn_b_out"],
            "md1_in": _MD1,
            "wg1": inputs["L1_ffn_w_gate"],
            "wu1": inputs["L1_ffn_w_up"],
            "wd1": inputs["L1_ffn_w_down"],
            "ident_f_in": _IDENT_F,
            "ident_b_in": _IDENT_B,
        })
    res = run_bass_kernel_spmd(nc, in_maps, core_ids=list(range(NC)))
    return res.results


def shard_tokens(x):
    out = []
    for c in range(NC):
        a, b = own_chunks(c)
        out.append(np.ascontiguousarray(
            np.concatenate([x[a * CH:(a + 1) * CH], x[b * CH:(b + 1) * CH]],
                           axis=0)))
    return out


def unshard_tokens(slices):
    x = np.empty((T, slices[0].shape[1]), dtype=slices[0].dtype)
    for c in range(NC):
        a, b = own_chunks(c)
        x[a * CH:(a + 1) * CH] = slices[c][:CH]
        x[b * CH:(b + 1) * CH] = slices[c][CH:]
    return x


_ROPE_C = [None] * NC
_ROPE_S = [None] * NC
for _c in range(NC):
    _ROPE_C[_c], _ROPE_S[_c] = _rope_tables(_c)


def kernel(**inputs):
    inputs = {kk: np.asarray(vv) for kk, vv in inputs.items()}
    x = np.ascontiguousarray(inputs["L0_attn_hidden_states"][0])
    shards = shard_tokens(x)
    resA = _run_A(inputs, shards)
    kT0, v0 = _gather_kv(resA, "kT_out", "vT_out")
    resB = _run_B(inputs, resA, kT0, v0)
    kT1, v1 = _gather_kv(resB, "k1T_out", "v1T_out")
    resC = _run_C(inputs, resB, kT1, v1)
    out = unshard_tokens([np.ascontiguousarray(r["out_tok"]) for r in resC])
    return out[None, :, :].astype(np.float32)
